# version 09
# baseline (speedup 1.0000x reference)
import numpy as np
import concourse.bass as bass
import concourse.mybir as mybir
from concourse.bass_utils import run_bass_kernel_spmd

F32 = mybir.dt.float32
BF16 = mybir.dt.bfloat16
AF = mybir.ActivationFunctionType
ALU = mybir.AluOpType
AX = mybir.AxisListType

T = 2048
D = 2048
F = 2048
NT = T // 128
ND = D // 128
NF = F // 128
NH = 16
NG = 4
NE = 8
QSCALE = float(1.0 / np.sqrt(128.0))
EPS = 1e-5
NEG = -1e30

ENGS = ("tensor", "vector", "scalar", "gpsimd", "sync")


class Ctr:
    def __init__(self, name):
        self.name = name
        self.n = 0
        self.h = None

    def bump(self, amt):
        self.n += amt
        return self.n


class Sched:
    def __init__(self):
        self.q = {e: [] for e in ENGS}

    def emit(self, eng, fn, waits=(), inc=None, amt=1):
        w = tuple((c, v) for (c, v) in waits if v and v > 0)
        m = inc.bump(amt) if inc is not None else None
        self.q[eng].append((w, fn, inc, amt))
        return m


def build_program(n_heads=NH, n_experts=NE, detect_races=True, dbg=False):
    nc = bass.Bass("TRN2", target_bir_lowering=False,
                   detect_race_conditions=detect_races)

    x_h = nc.dram_tensor("x", [T, D], F32, kind="ExternalInput").ap()
    wq_h = nc.dram_tensor("wq", [128, NH * ND * 128], F32, kind="ExternalInput").ap()
    wk_h = nc.dram_tensor("wk", [128, ND * 512], F32, kind="ExternalInput").ap()
    wv_h = nc.dram_tensor("wv", [128, ND * 512], F32, kind="ExternalInput").ap()
    wo_h = nc.dram_tensor("wo", [D, D], F32, kind="ExternalInput").ap()
    bo_h = nc.dram_tensor("bo", [1, D], F32, kind="ExternalInput").ap()
    wr_h = nc.dram_tensor("wr", [128, ND * 8], F32, kind="ExternalInput").ap()
    wg_h = nc.dram_tensor("wg", [NE, NF, 128, ND * 128], F32, kind="ExternalInput").ap()
    wu_h = nc.dram_tensor("wu", [NE, NF, 128, ND * 128], F32, kind="ExternalInput").ap()
    wd_h = nc.dram_tensor("wd", [NE, F, D], F32, kind="ExternalInput").ap()
    id_h = nc.dram_tensor("ident", [128, 128], F32, kind="ExternalInput").ap()
    mk_h = nc.dram_tensor("mask", [128, 128], F32, kind="ExternalInput").ap()
    on_h = nc.dram_tensor("ones", [1, 128], F32, kind="ExternalInput").ap()
    y_h = nc.dram_tensor("y", [T, D], F32, kind="ExternalOutput").ap()
    y1tok_h = nc.dram_tensor("y1tok", [T, D], BF16, kind="Internal").ap()
    acc_h = nc.dram_tensor("acc", [T, D], BF16,
                           kind="ExternalOutput" if dbg else "Internal").ap()
    if dbg:
        tA_h = nc.dram_tensor("tA", [128, 32768], BF16, kind="ExternalOutput").ap()
        tDd_h = nc.dram_tensor("tDd", [128, 16384], BF16, kind="ExternalOutput").ap()
        tB1_h = nc.dram_tensor("tB1", [128, 32768], BF16, kind="ExternalOutput").ap()
        tB2_h = nc.dram_tensor("tB2", [128, 32768], BF16, kind="ExternalOutput").ap()
        tW_h = nc.dram_tensor("tW", [128, 128], F32, kind="ExternalOutput").ap()
        tS1_h = nc.dram_tensor("tS1", [128, 4096], F32, kind="ExternalOutput").ap()
        tC_h = nc.dram_tensor("tC", [128, 4096], BF16, kind="ExternalOutput").ap()
        tST_h = nc.dram_tensor("tST", [128, 8], F32, kind="ExternalOutput").ap()

    sch = Sched()
    PE = Ctr("PE"); VE = Ctr("VE"); AC = Ctr("AC")
    CONST = Ctr("CONST")
    XLD = [Ctr("X0"), Ctr("X1"), Ctr("X2"), Ctr("X3")]
    WQL = [Ctr("WQ0"), Ctr("WQ1")]
    WOL = Ctr("WO")
    WGL = [Ctr("WG0"), Ctr("WG1")]
    WUL = [Ctr("WU0"), Ctr("WU1")]
    WDL = Ctr("WD")
    SYL = [Ctr("SY0"), Ctr("SY1")]
    STT = [Ctr(f"ST{i}") for i in range(NT)]
    ALL_CTRS = ([PE, VE, AC, CONST, WOL, WDL] + XLD + WQL + WGL + WUL + SYL + STT)

    from contextlib import ExitStack
    with ExitStack() as stack:
        A = stack.enter_context(nc.sbuf_tensor("A", [128, 32768], BF16))
        B = stack.enter_context(nc.sbuf_tensor("B", [128, 32768], BF16))
        C = stack.enter_context(nc.sbuf_tensor("C", [128, 16384], BF16))
        Dd = stack.enter_context(nc.sbuf_tensor("Dd", [128, 16384], BF16))
        SM = stack.enter_context(nc.sbuf_tensor("SM", [128, 7680], BF16))
        PA = stack.enter_context(nc.psum_tensor("PA", [128, 2048], F32))
        PB = stack.enter_context(nc.psum_tensor("PB", [128, 2048], F32))
        for c in ALL_CTRS:
            c.h = stack.enter_context(nc.semaphore("s" + c.name))
        block = stack.enter_context(nc.Block())
        Dd32 = Dd.bitcast(F32)
        SM32 = SM.bitcast(F32)
        A3d = A[:, :].rearrange("p (c f) -> p c f", f=2048)
        B3d = B[:, :].rearrange("p (c f) -> p c f", f=2048)
        C3dP = C[:, :].rearrange("p (c f) -> p c f", f=512)

        IDC = 4096
        MKBC = 4224
        DGC = 4480
        SEC32, RC32 = 2368, 2384
        ONC = 4800
        BOC = 4928
        WTC32 = 3488
        WRC = 7232
        LGC32, EXC32 = 3680, 3688
        E2C32, DEN32, M132 = 3696, 3697, 3698
        SELC32 = 3700
        S_SUM, S_SQ, S_MEAN, S_VAR, S_STD, S_RSTD, S_NMEAN, S_T1 = (
            3710, 3711, 3712, 3713, 3714, 3715, 3716, 3717)
        XSLOT = [0, 2048, 12288, 14336]
        XR32 = [0, 2048]
        WR32 = [4096, 6144]
        ACRD = [0, 2048]
        ACWR = [8192, 10240]
        WGB = [16384, 18432]
        WUB = [20480, 22528]
        QW = [12288, 14336]

        def gdma(ctr, dst, src, waits=()):
            return sch.emit("gpsimd", lambda g, d=dst, s=src: g.dma_start(out=d, in_=s),
                            waits, ctr, 16)

        def tslot(g):
            return (PA, g * 512) if g < 2 else (PB, (g - 2) * 512)

        m_xld = {}
        m_xt = {}
        m_xevac = {}
        m_v = {}
        m_vevac = {}
        last_tsl = {0: 0, 1: 0, 2: 0, 3: 0}

        m_xld[0] = gdma(XLD[0], C[:, XSLOT[0]:XSLOT[0] + 2048], x_h[0:128, :])
        m_xld[1] = gdma(XLD[1], C[:, XSLOT[1]:XSLOT[1] + 2048], x_h[128:256, :])
        gdma(CONST, SM[:, IDC:IDC + 128], id_h[:, :])
        gdma(CONST, SM[:, MKBC:MKBC + 128], mk_h[:, :])
        m_xld[2] = gdma(XLD[2], C[:, XSLOT[2]:XSLOT[2] + 2048], x_h[256:384, :])
        m_xld[3] = gdma(XLD[3], C[:, XSLOT[3]:XSLOT[3] + 2048], x_h[384:512, :])
        gdma(CONST, SM[0:1, ONC:ONC + 128], on_h[:, :])
        gdma(CONST, SM[0:1, BOC:BOC + 2048], bo_h[:, :])
        gdma(CONST, SM[:, WRC:WRC + 128], wr_h[:, :])
        gdma(CONST, B[:, 0:8192], wv_h[:, :])
        gdma(CONST, C[:, 4096:12288], wk_h[:, :])
        m_const = CONST.n

        for tt in range(NT):
            if tt >= 4:
                m_xld[tt] = gdma(XLD[tt % 4], C[:, XSLOT[tt % 4]:XSLOT[tt % 4] + 2048],
                                 x_h[tt * 128:(tt + 1) * 128, :],
                                 [(PE, m_xt[(tt - 4, 3)])])
            for g in range(4):
                ps, off = tslot(g)
                def mk_tr(t_, tt=tt, g=g, ps=ps, off=off):
                    ins = None
                    for j in range(4):
                        dc = g * 4 + j
                        ins = t_.matmul(ps[:, off + j * 128: off + (j + 1) * 128],
                                        C[:, XSLOT[tt % 4] + dc * 128:
                                          XSLOT[tt % 4] + (dc + 1) * 128],
                                        SM[:, IDC:IDC + 128], start=True, stop=True)
                    return ins
                m_xt[(tt, g)] = sch.emit("tensor", mk_tr,
                                         [(XLD[tt % 4], m_xld[tt]), (CONST, m_const),
                                          (VE, last_tsl[g])], PE)
                def mk_ev(v, tt=tt, g=g, ps=ps, off=off):
                    dst = A3d[:, g * 4:(g + 1) * 4, tt * 128:(tt + 1) * 128]
                    src = ps[:, off:off + 512].rearrange("p (c f) -> p c f", f=128)
                    return v.tensor_copy(dst, src)
                m_xevac[(tt, g)] = sch.emit("vector", mk_ev, [(PE, m_xt[(tt, g)])], VE)
                last_tsl[g] = m_xevac[(tt, g)]
            def mk_v(t_, tt=tt):
                ins = None
                for dc in range(ND):
                    ins = t_.matmul(PA[:, 1024:1536],
                                    A3d[:, dc, tt * 128:(tt + 1) * 128],
                                    B[:, dc * 512:(dc + 1) * 512],
                                    start=(dc == 0), stop=(dc == ND - 1))
                return ins
            m_v[tt] = sch.emit("tensor", mk_v,
                               [(VE, m_xevac[(tt, 3)]), (CONST, m_const),
                                (VE, m_vevac.get(tt - 1, 0))], PE)
            m_vevac[tt] = sch.emit(
                "vector",
                lambda v, tt=tt: v.tensor_copy(
                    Dd[:, 8192 + tt * 512: 8192 + (tt + 1) * 512], PA[:, 1024:1536]),
                [(PE, m_v[tt])], VE)

        prev_k = 0
        for g in range(NG):
            for ts in range(4):
                def mk_k(t_, g=g, ts=ts):
                    ins = None
                    for dc in range(ND):
                        ins = t_.matmul(PA[:, 1536:2048],
                                        C[:, 4096 + dc * 512 + g * 128:
                                          4096 + dc * 512 + (g + 1) * 128],
                                        A3d[:, dc, ts * 512:(ts + 1) * 512],
                                        start=(dc == 0), stop=(dc == ND - 1))
                    return ins
                m_k = sch.emit("tensor", mk_k,
                               [(CONST, m_const), (VE, m_xevac[(NT - 1, 3)]),
                                (VE, prev_k)], PE)
                prev_k = sch.emit(
                    "vector",
                    lambda v, g=g, ts=ts: v.tensor_copy(
                        Dd[:, g * 2048 + ts * 512: g * 2048 + (ts + 1) * 512],
                        PA[:, 1536:2048]),
                    [(PE, m_k)], VE)
        m_k_done = prev_k

        m_wq = {}
        m_q = {}
        m_qevac = {}
        prev_q = {0: 0, 1: 0}
        for h in range(n_heads):
            wqw = [(PE, m_q.get(h - 2, 0))]
            if h < 2:
                wqw.append((PE, m_xt[(NT - 2 + h, 3)]))
            m_wq[h] = gdma(WQL[h % 2], C[:, QW[h % 2]:QW[h % 2] + 2048],
                           wq_h[:, h * 2048:(h + 1) * 2048], wqw)
            for ts in range(4):
                po = 1024 + (ts % 2) * 512
                def mk_q(t_, h=h, ts=ts, po=po):
                    ins = None
                    for dc in range(ND):
                        ins = t_.matmul(PB[:, po:po + 512],
                                        C[:, QW[h % 2] + dc * 128:
                                          QW[h % 2] + (dc + 1) * 128],
                                        A3d[:, dc, ts * 512:(ts + 1) * 512],
                                        start=(dc == 0), stop=(dc == ND - 1))
                    return ins
                waits = [(WQL[h % 2], m_wq[h]), (VE, prev_q[ts % 2])]
                if h == 0 and ts == 0:
                    waits.append((PE, m_v[NT - 1]))
                m_q[h] = sch.emit("tensor", mk_q, waits, PE)
                m_qevac[(h, ts)] = sch.emit(
                    "vector",
                    lambda v, h=h, ts=ts, po=po: v.tensor_scalar(
                        B[:, h * 2048 + ts * 512: h * 2048 + (ts + 1) * 512],
                        PB[:, po:po + 512], QSCALE, None, ALU.mult),
                    [(PE, m_q[h])], VE)
                prev_q[ts % 2] = m_qevac[(h, ts)]
        m_q_all = m_q.get(n_heads - 1, 0)

        m_tapA = 0
        if dbg:
            m_tapA = sch.emit(
                "sync", lambda sy: sy.dma_start(out=tA_h[:, :], in_=A[:, :]),
                [(VE, m_xevac[(NT - 1, 3)]), (PE, m_q_all)], STT[0], 16)
        for dc in range(ND):
            w0_ = [(PE, m_q_all)] + ([(STT[0], m_tapA)] if dbg else [])
            gdma(WOL, A3d[:, dc, :], wo_h[dc * 128:(dc + 1) * 128, :],
                 w0_ if dc == 0 else ())
        m_wo_all = WOL.n

        m_exp = {}
        last_exp = 0
        last_probs_rd = {0: 0, 1: 0}
        t3free = {0: 0, 1: 0, 2: 0}
        m_ctx = 0
        m_ctx_evac = 0
        m_recip = {}
        m_diag = {}
        m_scmb = {}
        m_expchunk = {}

        def emit_scores(h, tt):
            g = h // 4
            kw = (tt + 1) * 128
            nch = (kw + 511) // 512
            m_pm = 0
            for ns in range(nch):
                w_ = min(512, kw - ns * 512)
                last = (ns == nch - 1)
                def mk_sc(t_, h=h, tt=tt, g=g, ns=ns, w_=w_, last=last, kw=kw):
                    ins = t_.matmul(
                        PA[:, ns * 512: ns * 512 + w_],
                        B[:, h * 2048 + tt * 128: h * 2048 + (tt + 1) * 128],
                        Dd[:, g * 2048 + ns * 512: g * 2048 + ns * 512 + w_],
                        start=True, stop=not last)
                    if last:
                        ins = t_.matmul(PA[:, kw - 128:kw], SM[:, IDC:IDC + 128],
                                        SM[:, MKBC:MKBC + 128], start=False, stop=True)
                    return ins
                if ns == 0:
                    w0 = [(AC, m_expchunk.get((tt % 2 ^ 1, 0), 0)),
                          (VE, m_qevac[(h, 3)]), (VE, m_k_done), (CONST, m_const)]
                else:
                    w0 = [(AC, m_expchunk.get((tt % 2 ^ 1, ns), 0))]
                m_sc = sch.emit("tensor", mk_sc, w0, PE)
                m_pm = sch.emit("vector",
                                lambda v, ns=ns, w_=w_: v.reduce_max(
                                    SM32[:, 3720 + ns:3721 + ns],
                                    PA[:, ns * 512: ns * 512 + w_], axis=AX.X),
                                [(PE, m_sc)], VE)
            m_mx = sch.emit("vector",
                            lambda v, nch=nch: v.reduce_max(
                                SM32[:, M132:M132 + 1], SM32[:, 3720:3720 + nch],
                                axis=AX.X, negate=True), [(VE, m_pm)], VE)
            pbase = 3724 + 4 * (tt % 2)
            m_e = 0
            for ns in range(nch):
                w_ = min(512, kw - ns * 512)
                ew = [(VE, m_mx)]
                if ns == 0:
                    ew += [(PE, last_probs_rd[tt % 2]),
                           (VE, m_scmb.get(tt % 2, 0))]
                m_e = sch.emit(
                    "scalar",
                    lambda s, tt=tt, ns=ns, w_=w_, pbase=pbase: s.activation(
                        SM[:, (tt % 2) * 2048 + ns * 512:
                           (tt % 2) * 2048 + ns * 512 + w_],
                        PA[:, ns * 512: ns * 512 + w_],
                        AF.Exp, bias=SM32[:, M132:M132 + 1],
                        accum_out=SM32[:, pbase + ns:pbase + ns + 1]),
                    ew, AC)
                m_expchunk[(tt % 2, ns)] = m_e
            m_exp[tt] = m_e
            m_scmb[tt % 2] = sch.emit(
                "vector",
                lambda v, tt=tt, nch=nch, pbase=pbase: v.reduce_sum(
                    SM32[:, SEC32 + tt:SEC32 + tt + 1],
                    SM32[:, pbase:pbase + nch], axis=AX.X),
                [(AC, m_e)], VE)
            m_recip[tt] = sch.emit(
                "vector",
                lambda v, tt=tt: v.reciprocal(
                    SM32[:, RC32 + tt:RC32 + tt + 1],
                    SM32[:, SEC32 + tt:SEC32 + tt + 1]),
                [(VE, m_scmb[tt % 2])], VE)
            m_diag[tt] = sch.emit(
                "scalar",
                lambda s, tt=tt: s.activation(
                    SM[:, DGC + (tt % 2) * 128: DGC + (tt % 2) * 128 + 128],
                    SM[:, IDC:IDC + 128], AF.Identity,
                    scale=SM32[:, RC32 + tt:RC32 + tt + 1]),
                [(VE, m_recip[tt])], AC)
            return m_e

        def emit_transposes(tt):
            nonlocal_last = 0
            ngr = (tt + 4) // 4
            for gr in range(ngr):
                sl = gr % 3
                k0 = gr * 4
                kn = min(4, tt + 1 - k0)
                j = tt % 4
                def mk_tp(t_, tt=tt, k0=k0, kn=kn, sl=sl):
                    ins = None
                    for jj in range(kn):
                        kt = k0 + jj
                        ins = t_.matmul(
                            PB[:, sl * 512 + jj * 128: sl * 512 + (jj + 1) * 128],
                            SM[:, (tt % 2) * 2048 + kt * 128:
                               (tt % 2) * 2048 + (kt + 1) * 128],
                            SM[:, DGC + (tt % 2) * 128:
                               DGC + (tt % 2) * 128 + 128],
                            start=True, stop=True)
                    return ins
                m_tg = sch.emit("tensor", mk_tp,
                                [(AC, m_diag[tt]), (VE, t3free[sl])], PE)
                nonlocal_last = m_tg
                def mk_te(v, k0=k0, kn=kn, sl=sl, j=j):
                    dst = C3dP[:, k0:k0 + kn, j * 128:(j + 1) * 128]
                    src = PB[:, sl * 512: sl * 512 + kn * 128].rearrange(
                        "p (c f) -> p c f", f=128)
                    return v.tensor_copy(dst, src)
                t3free[sl] = sch.emit("vector", mk_te, [(PE, m_tg)], VE)
            last_probs_rd[tt % 2] = nonlocal_last

        def emit_ctx(h, qb):
            nonlocal m_ctx, m_ctx_evac
            g = h // 4
            nkt = qb * 4 + 4
            def mk_cx(t_, g=g, qb=qb, nkt=nkt):
                ins = None
                for kt in range(nkt):
                    off = max(0, kt - qb * 4) * 128
                    ins = t_.matmul(
                        PB[:, 1536 + off:2048],
                        Dd[:, 8192 + kt * 512 + g * 128:
                           8192 + kt * 512 + (g + 1) * 128],
                        C[:, kt * 512 + off: kt * 512 + 512],
                        start=(kt == 0), stop=(kt == nkt - 1))
                return ins
            m_ctx = sch.emit("tensor", mk_cx,
                             [(VE, max(t3free.values())), (VE, m_vevac[NT - 1]),
                              (VE, m_ctx_evac)], PE)
            m_ctx_evac = sch.emit(
                "vector",
                lambda v, h=h, qb=qb: v.tensor_copy(
                    B[:, h * 2048 + qb * 512: h * 2048 + (qb + 1) * 512],
                    PB[:, 1536:2048]),
                [(PE, m_ctx)], VE)

        for h in range(n_heads):
            for tt in range(NT):
                last_exp = emit_scores(h, tt)
                if tt >= 1:
                    emit_transposes(tt - 1)
                    if (tt - 1) % 4 == 3:
                        emit_ctx(h, (tt - 1) // 4)
            emit_transposes(NT - 1)
            emit_ctx(h, 3)
        m_ctx_all = m_ctx_evac

        m_xres = {}
        m_y1st = {}
        m_y1tok_d = {}
        m_wo_mm = {}
        m_stt = {}
        m_sq = {}
        m_y1evac = {}
        m_yt_tile = {}
        last_y1sl = {0: 0, 1: 0, 2: 0, 3: 0}

        def emit_ln1(waits):
            e1 = sch.emit("vector", lambda v: (
                v.tensor_scalar(SM32[:, S_MEAN:S_MEAN + 1], SM32[:, S_SUM:S_SUM + 1],
                                1.0 / D, None, ALU.mult),
                v.tensor_scalar(SM32[:, S_T1:S_T1 + 1], SM32[:, S_SQ:S_SQ + 1],
                                1.0 / D, None, ALU.mult))[-1], waits, VE)
            e2 = sch.emit("vector", lambda v: v.tensor_tensor(
                SM32[:, S_VAR:S_VAR + 1], SM32[:, S_MEAN:S_MEAN + 1],
                SM32[:, S_MEAN:S_MEAN + 1], ALU.mult), [(VE, e1)], VE)
            e3 = sch.emit("vector", lambda v: v.tensor_tensor(
                SM32[:, S_T1:S_T1 + 1], SM32[:, S_T1:S_T1 + 1],
                SM32[:, S_VAR:S_VAR + 1], ALU.subtract), [(VE, e2)], VE)
            return sch.emit("vector", lambda v: v.tensor_scalar(
                SM32[:, S_T1:S_T1 + 1], SM32[:, S_T1:S_T1 + 1],
                EPS, None, ALU.add), [(VE, e3)], VE)

        def mk_sqrt(s):
            return s.activation(SM32[:, S_STD:S_STD + 1], SM32[:, S_T1:S_T1 + 1],
                                AF.Sqrt)

        for tt in range(NT):
            m_xres[tt] = sch.emit(
                "sync",
                lambda sy, tt=tt: sy.dma_start(
                    out=Dd32[:, XR32[tt % 2]:XR32[tt % 2] + 2048],
                    in_=x_h[tt * 128:(tt + 1) * 128, :]),
                [(VE, m_stt.get(tt - 2, 0))] + ([(VE, m_ctx_all)] if tt < 2 else [])
                + ([(STT[1], m_tapD)] if (dbg and tt < 2) else []),
                SYL[tt % 2], 16)
            def mk_wo(t_, tt=tt):
                ins = None
                for dc in range(n_heads):
                    for nb in range(4):
                        ins = t_.matmul(PA[:, nb * 512:(nb + 1) * 512],
                                        B3d[:, dc, tt * 128:(tt + 1) * 128],
                                        A3d[:, dc, nb * 512:(nb + 1) * 512],
                                        start=(dc == 0), stop=False)
                for nb in range(4):
                    ins = t_.matmul(PA[:, nb * 512:(nb + 1) * 512],
                                    SM[0:1, ONC:ONC + 128],
                                    SM[0:1, BOC + nb * 512: BOC + (nb + 1) * 512],
                                    start=False, stop=True)
                return ins
            waits = [(VE, m_ctx_all), (WOL, m_wo_all), (CONST, m_const),
                     (VE, m_stt.get(tt - 1, 0))]
            if tt == 0:
                waits.append((AC, last_exp))
            m_wo_mm[tt] = sch.emit("tensor", mk_wo, waits, PE)
            m_stt[tt] = sch.emit(
                "vector",
                lambda v, tt=tt: v.scalar_tensor_tensor(
                    Dd32[:, WR32[0]:WR32[0] + 2048], PA[:, 0:2048], 1.0,
                    Dd32[:, XR32[tt % 2]:XR32[tt % 2] + 2048], ALU.mult, ALU.add,
                    accum_out=SM32[:, S_SUM:S_SUM + 1]),
                [(PE, m_wo_mm[tt]), (SYL[tt % 2], m_xres[tt])], VE)
            m_sq[tt] = sch.emit(
                "scalar",
                lambda s: s.activation(Dd32[:, WR32[1]:WR32[1] + 2048],
                                       Dd32[:, WR32[0]:WR32[0] + 2048],
                                       AF.Square, accum_out=SM32[:, S_SQ:S_SQ + 1]),
                [(VE, m_stt[tt])], AC)
            m_v1 = emit_ln1([(AC, m_sq[tt])])
            m_sd = sch.emit("scalar", mk_sqrt, [(VE, m_v1)], AC)
            def mk_ln2a(v):
                v.reciprocal(SM32[:, S_RSTD:S_RSTD + 1], SM32[:, S_STD:S_STD + 1])
                return v.tensor_scalar(SM32[:, S_NMEAN:S_NMEAN + 1],
                                       SM32[:, S_MEAN:S_MEAN + 1], -1.0, None, ALU.mult)

            def mk_ln2b(v):
                return v.tensor_tensor(SM32[:, S_NMEAN:S_NMEAN + 1],
                                       SM32[:, S_NMEAN:S_NMEAN + 1],
                                       SM32[:, S_RSTD:S_RSTD + 1], ALU.mult)
            m_ln2a = sch.emit("vector", mk_ln2a, [(AC, m_sd)], VE)
            m_ln2v = sch.emit("vector", mk_ln2b, [(VE, m_ln2a)], VE)
            m_y1tok_d[tt] = m_y1tok = sch.emit(
                "scalar",
                lambda s, tt=tt: s.activation(
                    C[:, XSLOT[tt % 2]:XSLOT[tt % 2] + 2048],
                    Dd32[:, WR32[0]:WR32[0] + 2048], AF.Identity,
                    scale=SM32[:, S_RSTD:S_RSTD + 1],
                    bias=SM32[:, S_NMEAN:S_NMEAN + 1]),
                [(VE, m_ln2v), (PE, m_yt_tile.get(tt - 2, 0)),
                 (SYL[tt % 2], m_y1st.get(tt - 2, 0))], AC)
            m_y1st[tt] = sch.emit(
                "sync",
                lambda sy, tt=tt: sy.dma_start(
                    out=y1tok_h[tt * 128:(tt + 1) * 128, :],
                    in_=C[:, XSLOT[tt % 2]:XSLOT[tt % 2] + 2048]),
                [(AC, m_y1tok)], SYL[tt % 2], 16)
            def emit_y1t(tt2):
                for g4 in range(4):
                    ps, off = PB, g4 * 512
                    def mk_yt(t_, tt2=tt2, g4=g4, ps=ps, off=off):
                        ins = None
                        for jj in range(4):
                            dc = g4 * 4 + jj
                            ins = t_.matmul(ps[:, off + jj * 128: off + (jj + 1) * 128],
                                            C[:, XSLOT[tt2 % 2] + dc * 128:
                                              XSLOT[tt2 % 2] + (dc + 1) * 128],
                                            SM[:, IDC:IDC + 128], start=True, stop=True)
                        return ins
                    m_yt = sch.emit("tensor", mk_yt,
                                    [(AC, m_y1tok_d[tt2]), (VE, last_y1sl[g4])], PE)
                    def mk_ye(v, tt2=tt2, g4=g4, ps=ps, off=off):
                        dst = B3d[:, g4 * 4:(g4 + 1) * 4, tt2 * 128:(tt2 + 1) * 128]
                        src = ps[:, off:off + 512].rearrange("p (c f) -> p c f", f=128)
                        return v.tensor_copy(dst, src)
                    m_y1evac[(tt2, g4)] = sch.emit("vector", mk_ye, [(PE, m_yt)], VE)
                    last_y1sl[g4] = m_y1evac[(tt2, g4)]
                m_yt_tile[tt2] = m_yt
            if tt >= 1:
                emit_y1t(tt - 1)
            if tt == NT - 1:
                emit_y1t(NT - 1)
        m_y1_done = m_y1evac[(NT - 1, 3)]
        m_yt_last = m_yt_tile[NT - 1]

        m_w = {}
        prev_r1 = 0
        for tt in range(NT):
            def mk_rl(t_, tt=tt):
                ins = None
                for dc in range(ND):
                    ins = t_.matmul(PB[:, 0:8],
                                    B3d[:, dc, tt * 128:(tt + 1) * 128],
                                    SM[:, WRC + dc * 8: WRC + (dc + 1) * 8],
                                    start=(dc == 0), stop=(dc == ND - 1))
                return ins
            rlw = [(VE, m_y1evac[(tt, 3)]), (CONST, m_const), (AC, prev_r1)]
            if tt == 0:
                rlw.append((VE, m_y1_done))
            m_rl = sch.emit("tensor", mk_rl, rlw, PE)
            m_r1 = sch.emit("vector",
                            lambda v: v.reduce_max(SM32[:, M132:M132 + 1], PB[:, 0:8],
                                                   axis=AX.X, negate=True),
                            [(PE, m_rl)], VE)
            m_re = sch.emit("scalar",
                            lambda s: s.activation(SM32[:, EXC32:EXC32 + 8],
                                                   PB[:, 0:8], AF.Exp,
                                                   bias=SM32[:, M132:M132 + 1]),
                            [(VE, m_r1)], AC)
            prev_r1 = m_re
            m_0 = sch.emit("vector", lambda v: v.tensor_scalar(
                SM32[:, SELC32:SELC32 + 8], SM32[:, EXC32:EXC32 + 8],
                1.0, None, ALU.is_lt), [(AC, m_re)], VE)
            m_1 = sch.emit("vector", lambda v: v.tensor_tensor(
                SM32[:, SELC32:SELC32 + 8], SM32[:, SELC32:SELC32 + 8],
                SM32[:, EXC32:EXC32 + 8], ALU.mult), [(VE, m_0)], VE)
            m_2 = sch.emit("vector", lambda v: v.reduce_max(
                SM32[:, E2C32:E2C32 + 1], SM32[:, SELC32:SELC32 + 8],
                axis=AX.X), [(VE, m_1)], VE)
            m_3 = sch.emit("vector", lambda v: v.tensor_scalar(
                SM32[:, DEN32:DEN32 + 1], SM32[:, E2C32:E2C32 + 1],
                1.0, None, ALU.add), [(VE, m_2)], VE)
            m_4 = sch.emit("vector", lambda v: v.reciprocal(
                SM32[:, DEN32:DEN32 + 1], SM32[:, DEN32:DEN32 + 1]), [(VE, m_3)], VE)
            m_5 = sch.emit("vector", lambda v: v.tensor_scalar(
                SM32[:, SELC32:SELC32 + 8], SM32[:, EXC32:EXC32 + 8],
                SM32[:, E2C32:E2C32 + 1], None, ALU.is_ge), [(VE, m_4)], VE)
            m_6 = sch.emit("vector", lambda v: v.tensor_tensor(
                SM32[:, SELC32:SELC32 + 8], SM32[:, SELC32:SELC32 + 8],
                SM32[:, EXC32:EXC32 + 8], ALU.mult), [(VE, m_5)], VE)
            m_w[tt] = sch.emit("vector", lambda v, tt=tt: v.tensor_scalar(
                SM32[:, WTC32 + tt * 8: WTC32 + (tt + 1) * 8],
                SM32[:, SELC32:SELC32 + 8],
                SM32[:, DEN32:DEN32 + 1], None, ALU.mult), [(VE, m_6)], VE)
        m_w_done = m_w[NT - 1]
        m_tapC = 0
        if dbg:
            sch.emit("sync", lambda sy: sy.dma_start(out=tB2_h[:, :], in_=B[:, :]),
                     [(VE, m_y1_done)], STT[3], 16)
            sch.emit("sync", lambda sy: sy.dma_start(out=tS1_h[:, :],
                                                     in_=Dd32[:, 4096:8192]),
                     [(VE, m_y1_done)], STT[5], 16)
            m_tapC = sch.emit("sync",
                              lambda sy: sy.dma_start(out=tC_h[:, :], in_=C[:, 0:4096]),
                              [(AC, m_y1tok)], STT[6], 16)
            sch.emit("sync", lambda sy: sy.dma_start(out=tST_h[:, :],
                                                     in_=SM32[:, 3710:3718]),
                     [(VE, m_y1_done)], STT[7], 16)
            sch.emit("sync", lambda sy: sy.dma_start(out=tW_h[:, :], in_=SM32[:, WTC32:WTC32 + 128]),
                     [(VE, m_w_done)], STT[4], 16)

        m_gu = {}
        m_silu = {}
        m_hm = {}
        m_eb = {}
        m_accld = {}
        m_bstt = {}
        m_accst = {}
        m_wdld = {}
        m_f1add = {}
        m_y2st = {}
        m_eb_last = 0

        def prev_key(e, fh, tt):
            k = tt - 2
            if k >= 0:
                return (e, fh, k)
            if fh == 1:
                return (e, 0, NT + k)
            if e > 0:
                return (e - 1, 1, NT + k)
            return None

        for e in range(n_experts):
            for fh in range(2):
                half0 = (e == 0 and fh == 0)
                is_last = (e == n_experts - 1 and fh == 1)
                for lfc in range(8):
                    fc = fh * 8 + lfc
                    if fc >= 2:
                        wgw = [(PE, m_gu[(e, fc - 2, 1)])]
                    elif e > 0:
                        wgw = [(PE, m_gu[(e - 1, 14 + fc, 1)])]
                    else:
                        wgw = [(PE, m_wo_mm[NT - 1])]
                    m_wgld = gdma(WGL[fc % 2], A[:, WGB[fc % 2]:WGB[fc % 2] + 2048],
                                  wg_h[e, fc, :, :], wgw)
                    m_wuld = gdma(WUL[fc % 2], A[:, WUB[fc % 2]:WUB[fc % 2] + 2048],
                                  wu_h[e, fc, :, :])
                    if lfc >= 2:
                        lw = lfc - 2
                        wdw = [(PE, m_eb_last)]
                        if half0:
                            wdw.append((PE, m_yt_last))
                            if lw < 2:
                                wdw.append((SYL[lw % 2], m_y1st[14 + (lw % 2)]))
                            if dbg:
                                wdw.append((STT[6], m_tapC))
                        m_wdld[(e, fh, lw)] = gdma(
                            WDL, C[:, lw * 2048:(lw + 1) * 2048],
                            wd_h[e, (fh * 8 + lw) * 128:(fh * 8 + lw + 1) * 128, :], wdw)
                    for th in range(2):
                        ps = PA if th == 0 else PB
                        def mk_gu(t_, fc=fc, th=th, ps=ps):
                            ins = None
                            for dc in range(ND):
                                for s2 in range(2):
                                    c0 = th * 1024 + s2 * 512
                                    ins = t_.matmul(
                                        ps[:, s2 * 512:(s2 + 1) * 512],
                                        A[:, WGB[fc % 2] + dc * 128:
                                          WGB[fc % 2] + (dc + 1) * 128],
                                        B3d[:, dc, c0:c0 + 512],
                                        start=(dc == 0), stop=(dc == ND - 1))
                                for s2 in range(2):
                                    c0 = th * 1024 + s2 * 512
                                    ins = t_.matmul(
                                        ps[:, 1024 + s2 * 512: 1024 + (s2 + 1) * 512],
                                        A[:, WUB[fc % 2] + dc * 128:
                                          WUB[fc % 2] + (dc + 1) * 128],
                                        B3d[:, dc, c0:c0 + 512],
                                        start=(dc == 0), stop=(dc == ND - 1))
                            return ins
                        waits = [(WGL[fc % 2], m_wgld), (WUL[fc % 2], m_wuld),
                                 (AC, m_silu.get((e, fc - 1, th), 0)),
                                 (VE, m_hm.get((e, fc - 1, th), 0))]
                        if lfc == 0:
                            waits.append((PE, m_eb_last))
                            ptt = NT - 2 + th
                            prev_b = m_bstt.get((e, fh - 1, ptt),
                                                m_bstt.get((e - 1, 1, ptt), 0))
                            waits.append((VE, prev_b))
                            if half0:
                                waits.append((VE, m_w_done))
                        m_gu[(e, fc, th)] = sch.emit("tensor", mk_gu, waits, PE)
                        m_silu[(e, fc, th)] = sch.emit(
                            "scalar",
                            lambda s, th=th, ps=ps: s.activation(
                                SM32[:, th * 1024:(th + 1) * 1024], ps[:, 0:1024],
                                AF.Silu),
                            [(PE, m_gu[(e, fc, th)]),
                             (VE, m_hm.get((e, fc - 1, th), 0))], AC)
                        m_hm[(e, fc, th)] = sch.emit(
                            "vector",
                            lambda v, fc=fc, th=th, ps=ps: v.tensor_tensor(
                                A[:, (fc % 8) * 2048 + th * 1024:
                                  (fc % 8) * 2048 + (th + 1) * 1024],
                                SM32[:, th * 1024:(th + 1) * 1024],
                                ps[:, 1024:2048], ALU.mult),
                            [(AC, m_silu[(e, fc, th)])], VE)
                for lw in (6, 7):
                    wdw = [(PE, m_eb_last)]
                    if half0:
                        wdw.append((PE, m_yt_last))
                        if dbg:
                            wdw.append((STT[6], m_tapC))
                    m_wdld[(e, fh, lw)] = gdma(
                        WDL, C[:, lw * 2048:(lw + 1) * 2048],
                        wd_h[e, (fh * 8 + lw) * 128:(fh * 8 + lw + 1) * 128, :], wdw)
                for tt in range(NT):
                    eps_ = PA if tt % 2 == 0 else PB
                    ops_ = PB if tt % 2 == 0 else PA
                    def mk_eb(t_, tt=tt, eps_=eps_):
                        ins = None
                        for lfc in range(8):
                            for nb in range(4):
                                ins = t_.matmul(
                                    eps_[:, nb * 512:(nb + 1) * 512],
                                    A[:, lfc * 2048 + tt * 128:
                                      lfc * 2048 + (tt + 1) * 128],
                                    C[:, lfc * 2048 + nb * 512:
                                      lfc * 2048 + (nb + 1) * 512],
                                    start=(lfc == 0), stop=(lfc == 7))
                        return ins
                    waits = [(VE, m_hm[(e, fh * 8 + 7, 1)]),
                             (WDL, m_wdld[(e, fh, 7)]),
                             (VE, m_bstt.get((e, fh, tt - 2), 0))]
                    if tt < 2:
                        th = tt % 2
                        waits += [(AC, m_silu[(e, fh * 8 + 7, th)]),
                                  (VE, m_hm[(e, fh * 8 + 7, th)])]
                    m_eb[(e, fh, tt)] = sch.emit("tensor", mk_eb, waits, PE)
                    first = half0
                    pk = prev_key(e, fh, tt)
                    if not first:
                        ldw = [(VE, m_bstt.get(pk, 0) if pk else 0),
                               (STT[tt], m_accst.get((e, fh - 1, tt),
                                    m_accst.get((e - 1, 1, tt), 0)))]
                        if is_last:
                            ldw.append((STT[(tt - 2) % NT], m_y2st.get(tt - 2, 0)))
                        m_accld[(e, fh, tt)] = sch.emit(
                            "sync",
                            lambda sy, tt=tt: sy.dma_start(
                                out=Dd[:, ACRD[tt % 2]:ACRD[tt % 2] + 2048],
                                in_=acc_h[tt * 128:(tt + 1) * 128, :]),
                            ldw, SYL[tt % 2], 16)
                    wc0 = WTC32 + tt * 8 + e
                    if first:
                        def mk_st(v, tt=tt, eps_=eps_, wc0=wc0):
                            return v.tensor_scalar(
                                Dd[:, ACWR[tt % 2]:ACWR[tt % 2] + 2048],
                                eps_[:, 0:2048], SM32[:, wc0:wc0 + 1], None, ALU.mult)
                        stw = [(PE, m_eb[(e, fh, tt)]),
                               (STT[(tt - 2) % NT],
                                m_accst.get(pk, 0) if pk else 0),
                               (VE, m_w[tt])]
                        if tt < 2:
                            stw.append((AC, m_sq[NT - 1]))
                            if dbg:
                                stw.append((STT[5], 16))
                        m_bstt[(e, fh, tt)] = sch.emit("vector", mk_st, stw, VE)
                    else:
                        def mk_st(v, tt=tt, eps_=eps_, wc0=wc0, is_last=is_last):
                            out_ap = (Dd32[:, WR32[tt % 2]:WR32[tt % 2] + 2048]
                                      if is_last else
                                      Dd[:, ACWR[tt % 2]:ACWR[tt % 2] + 2048])
                            return v.scalar_tensor_tensor(
                                out_ap,
                                eps_[:, 0:2048], SM32[:, wc0:wc0 + 1],
                                Dd[:, ACRD[tt % 2]:ACRD[tt % 2] + 2048],
                                ALU.mult, ALU.add)
                        stw = [(PE, m_eb[(e, fh, tt)]),
                               (SYL[tt % 2], m_accld[(e, fh, tt)]),
                               (STT[(tt - 2) % NT],
                                m_accst.get(pk, 0) if pk else 0),
                               (VE, m_w[tt])]
                        if is_last:
                            stw.append((STT[(tt - 2) % NT], m_y2st.get(tt - 2, 0)))
                        m_bstt[(e, fh, tt)] = sch.emit("vector", mk_st, stw, VE)
                    if not is_last:
                        m_accst[(e, fh, tt)] = sch.emit(
                            "sync",
                            lambda sy, tt=tt: sy.dma_start(
                                out=acc_h[tt * 128:(tt + 1) * 128, :],
                                in_=Dd[:, ACWR[tt % 2]:ACWR[tt % 2] + 2048]),
                            [(VE, m_bstt[(e, fh, tt)])], STT[tt], 16)
                    else:
                        Y1SL = [24576, 26624]
                        m_y1ld = sch.emit(
                            "sync",
                            lambda sy, tt=tt: sy.dma_start(
                                out=A[:, Y1SL[tt % 2]:Y1SL[tt % 2] + 2048],
                                in_=y1tok_h[tt * 128:(tt + 1) * 128, :]),
                            [(SYL[tt % 2], m_y1st[tt]),
                             (VE, m_f1add.get(tt - 2, 0))], SYL[tt % 2], 16)
                        def mk_fa(v, tt=tt):
                            return v.scalar_tensor_tensor(
                                Dd32[:, WR32[tt % 2]:WR32[tt % 2] + 2048],
                                Dd32[:, WR32[tt % 2]:WR32[tt % 2] + 2048], 1.0,
                                A[:, Y1SL[tt % 2]:Y1SL[tt % 2] + 2048],
                                ALU.mult, ALU.add,
                                accum_out=SM32[:, S_SUM:S_SUM + 1])
                        m_f1add[tt] = sch.emit(
                            "vector", mk_fa,
                            [(VE, m_bstt[(e, fh, tt)]),
                             (SYL[tt % 2], m_y1ld)], VE)
                        m_fs = m_f1add[tt]
                        m_fsq = sch.emit(
                            "scalar",
                            lambda s, tt=tt: s.activation(
                                Dd32[:, XR32[tt % 2]:XR32[tt % 2] + 2048],
                                Dd32[:, WR32[tt % 2]:WR32[tt % 2] + 2048],
                                AF.Square, accum_out=SM32[:, S_SQ:S_SQ + 1]),
                            [(VE, m_fs)], AC)
                        m_fv1 = emit_ln1([(AC, m_fsq)])
                        m_fsd = sch.emit("scalar", mk_sqrt, [(VE, m_fv1)], AC)
                        m_flna = sch.emit("vector", mk_ln2a, [(AC, m_fsd)], VE)
                        m_flnv = sch.emit("vector", mk_ln2b, [(VE, m_flna)], VE)
                        m_fy2 = sch.emit(
                            "scalar",
                            lambda s, tt=tt: s.activation(
                                Dd32[:, XR32[tt % 2]:XR32[tt % 2] + 2048],
                                Dd32[:, WR32[tt % 2]:WR32[tt % 2] + 2048], AF.Identity,
                                scale=SM32[:, S_RSTD:S_RSTD + 1],
                                bias=SM32[:, S_NMEAN:S_NMEAN + 1]),
                            [(VE, m_flnv)], AC)
                        m_y2st[tt] = sch.emit(
                            "sync",
                            lambda sy, tt=tt: sy.dma_start(
                                out=y_h[tt * 128:(tt + 1) * 128, :],
                                in_=Dd32[:, XR32[tt % 2]:XR32[tt % 2] + 2048]),
                            [(AC, m_fy2)], STT[tt], 16)
                m_eb_last = m_eb[(e, fh, NT - 1)]

        sch.emit("sync", lambda sy: sy.nop(),
                 [(STT[i], STT[i].n) for i in range(NT)])

        def replay(engname):
            def run(eng):
                seen = {}
                for waits, fn, inc, amt in sch.q[engname]:
                    for (c, v) in waits:
                        if v > seen.get(c, 0):
                            eng.wait_ge(c.h, v)
                            seen[c] = v
                    ins = fn(eng)
                    if inc is not None:
                        assert ins is not None
                        ins.then_inc(inc.h, amt)
            return run

        block.tensor(replay("tensor"))
        block.vector(replay("vector"))
        block.scalar(replay("scalar"))
        block.gpsimd(replay("gpsimd"))
        block.sync(replay("sync"))

    return nc


def _relayout(wq, wk, wv, wr, wg, wu):
    wq_r = np.ascontiguousarray(
        wq.reshape(ND, 128, NH, 128).transpose(1, 2, 0, 3).reshape(128, NH * ND * 128))
    wk_r = np.ascontiguousarray(
        wk.reshape(ND, 128, 512).transpose(1, 0, 2).reshape(128, ND * 512))
    wv_r = np.ascontiguousarray(
        wv.reshape(ND, 128, 512).transpose(1, 0, 2).reshape(128, ND * 512))
    wr_r = np.ascontiguousarray(
        wr.reshape(ND, 128, 8).transpose(1, 0, 2).reshape(128, ND * 8))
    wg_r = np.ascontiguousarray(
        wg.reshape(NE, ND, 128, NF, 128).transpose(0, 3, 2, 1, 4)
        .reshape(NE, NF, 128, ND * 128))
    wu_r = np.ascontiguousarray(
        wu.reshape(NE, ND, 128, NF, 128).transpose(0, 3, 2, 1, 4)
        .reshape(NE, NF, 128, ND * 128))
    return wq_r, wk_r, wv_r, wr_r, wg_r, wu_r


def kernel(L0_attn_x, L0_attn_Wq, L0_attn_Wk, L0_attn_Wv, L0_attn_Wo, L0_attn_bo,
           L0_ffn_Wr, L0_ffn_Wg, L0_ffn_Wu, L0_ffn_Wd):
    x = np.ascontiguousarray(np.asarray(L0_attn_x, dtype=np.float32))
    wq = np.ascontiguousarray(np.asarray(L0_attn_Wq, dtype=np.float32))
    wk = np.ascontiguousarray(np.asarray(L0_attn_Wk, dtype=np.float32))
    wv = np.ascontiguousarray(np.asarray(L0_attn_Wv, dtype=np.float32))
    wo = np.ascontiguousarray(np.asarray(L0_attn_Wo, dtype=np.float32))
    bo = np.ascontiguousarray(np.asarray(L0_attn_bo, dtype=np.float32)).reshape(1, D)
    wr = np.ascontiguousarray(np.asarray(L0_ffn_Wr, dtype=np.float32))
    wg = np.ascontiguousarray(np.asarray(L0_ffn_Wg, dtype=np.float32))
    wu = np.ascontiguousarray(np.asarray(L0_ffn_Wu, dtype=np.float32))
    wd = np.ascontiguousarray(np.asarray(L0_ffn_Wd, dtype=np.float32))

    wq_r, wk_r, wv_r, wr_r, wg_r, wu_r = _relayout(wq, wk, wv, wr, wg, wu)
    ident = np.eye(128, dtype=np.float32)
    mask = np.where(np.arange(128)[:, None] < np.arange(128)[None, :], NEG,
                    0.0).astype(np.float32)
    ones = np.ones((1, 128), dtype=np.float32)

    nc = build_program()
    in_maps = []
    for c in range(8):
        in_maps.append({
            "x": np.ascontiguousarray(x[c]),
            "wq": wq_r, "wk": wk_r, "wv": wv_r, "wo": wo, "bo": bo,
            "wr": wr_r, "wg": wg_r, "wu": wu_r, "wd": wd,
            "ident": ident, "mask": mask, "ones": ones,
        })
    res = run_bass_kernel_spmd(nc, in_maps, core_ids=list(range(8)))
    return np.stack([res.results[c]["y"] for c in range(8)], axis=0)


if __name__ == "__main__":
    nc = build_program()
    print("instructions:", nc.n_instructions())


# version 10
# speedup vs baseline: 1.1225x; 1.1225x over previous
import numpy as np
import concourse.bass as bass
import concourse.mybir as mybir
from concourse.bass_utils import run_bass_kernel_spmd

F32 = mybir.dt.float32
BF16 = mybir.dt.bfloat16
AF = mybir.ActivationFunctionType
ALU = mybir.AluOpType
AX = mybir.AxisListType

T = 2048
D = 2048
F = 2048
NT = T // 128
ND = D // 128
NF = F // 128
NH = 16
NG = 4
NE = 8
QSCALE = float(1.0 / np.sqrt(128.0))
EPS = 1e-5
NEG = -1e30

ENGS = ("tensor", "vector", "scalar", "gpsimd", "sync")


class Ctr:
    def __init__(self, name):
        self.name = name
        self.n = 0
        self.h = None

    def bump(self, amt):
        self.n += amt
        return self.n


class Sched:
    def __init__(self):
        self.q = {e: [] for e in ENGS}

    def emit(self, eng, fn, waits=(), inc=None, amt=1):
        w = tuple((c, v) for (c, v) in waits if v and v > 0)
        m = inc.bump(amt) if inc is not None else None
        self.q[eng].append((w, fn, inc, amt))
        return m


def build_program(n_heads=NH, n_experts=NE, detect_races=True, dbg=False):
    nc = bass.Bass("TRN2", target_bir_lowering=False,
                   detect_race_conditions=detect_races)

    x_h = nc.dram_tensor("x", [T, D], F32, kind="ExternalInput").ap()
    wq_h = nc.dram_tensor("wq", [128, NH * ND * 128], F32, kind="ExternalInput").ap()
    wk_h = nc.dram_tensor("wk", [128, ND * 512], F32, kind="ExternalInput").ap()
    wv_h = nc.dram_tensor("wv", [128, ND * 512], F32, kind="ExternalInput").ap()
    wo_h = nc.dram_tensor("wo", [D, D], F32, kind="ExternalInput").ap()
    bo_h = nc.dram_tensor("bo", [1, D], F32, kind="ExternalInput").ap()
    wr_h = nc.dram_tensor("wr", [128, ND * 8], F32, kind="ExternalInput").ap()
    wg_h = nc.dram_tensor("wg", [NE, NF, 128, ND * 128], F32, kind="ExternalInput").ap()
    wu_h = nc.dram_tensor("wu", [NE, NF, 128, ND * 128], F32, kind="ExternalInput").ap()
    wd_h = nc.dram_tensor("wd", [NE, F, D], F32, kind="ExternalInput").ap()
    id_h = nc.dram_tensor("ident", [128, 128], F32, kind="ExternalInput").ap()
    mk_h = nc.dram_tensor("mask", [128, 128], F32, kind="ExternalInput").ap()
    on_h = nc.dram_tensor("ones", [1, 128], F32, kind="ExternalInput").ap()
    y_h = nc.dram_tensor("y", [T, D], F32, kind="ExternalOutput").ap()
    y1tok_h = nc.dram_tensor("y1tok", [T, D], BF16, kind="Internal").ap()
    acc_h = nc.dram_tensor("acc", [T, D], BF16,
                           kind="ExternalOutput" if dbg else "Internal").ap()
    if dbg:
        tA_h = nc.dram_tensor("tA", [128, 32768], BF16, kind="ExternalOutput").ap()
        tDd_h = nc.dram_tensor("tDd", [128, 16384], BF16, kind="ExternalOutput").ap()
        tB1_h = nc.dram_tensor("tB1", [128, 32768], BF16, kind="ExternalOutput").ap()
        tB2_h = nc.dram_tensor("tB2", [128, 32768], BF16, kind="ExternalOutput").ap()
        tW_h = nc.dram_tensor("tW", [128, 128], F32, kind="ExternalOutput").ap()
        tS1_h = nc.dram_tensor("tS1", [128, 4096], F32, kind="ExternalOutput").ap()
        tC_h = nc.dram_tensor("tC", [128, 4096], BF16, kind="ExternalOutput").ap()
        tST_h = nc.dram_tensor("tST", [128, 8], F32, kind="ExternalOutput").ap()

    sch = Sched()
    PE = Ctr("PE"); VE = Ctr("VE"); AC = Ctr("AC")
    CONST = Ctr("CONST")
    XLD = [Ctr("X0"), Ctr("X1"), Ctr("X2"), Ctr("X3")]
    WQL = [Ctr("WQ0"), Ctr("WQ1")]
    WOL = Ctr("WO")
    WGL = [Ctr("WG0"), Ctr("WG1")]
    WUL = [Ctr("WU0"), Ctr("WU1")]
    WDL = Ctr("WD")
    SYL = [Ctr("SY0"), Ctr("SY1")]
    STT = [Ctr(f"ST{i}") for i in range(NT)]
    ALL_CTRS = ([PE, VE, AC, CONST, WOL, WDL] + XLD + WQL + WGL + WUL + SYL + STT)

    from contextlib import ExitStack
    with ExitStack() as stack:
        A = stack.enter_context(nc.sbuf_tensor("A", [128, 32768], BF16))
        B = stack.enter_context(nc.sbuf_tensor("B", [128, 32768], BF16))
        C = stack.enter_context(nc.sbuf_tensor("C", [128, 16384], BF16))
        Dd = stack.enter_context(nc.sbuf_tensor("Dd", [128, 16384], BF16))
        SM = stack.enter_context(nc.sbuf_tensor("SM", [128, 7680], BF16))
        PA = stack.enter_context(nc.psum_tensor("PA", [128, 2048], F32))
        PB = stack.enter_context(nc.psum_tensor("PB", [128, 2048], F32))
        for c in ALL_CTRS:
            c.h = stack.enter_context(nc.semaphore("s" + c.name))
        block = stack.enter_context(nc.Block())
        Dd32 = Dd.bitcast(F32)
        SM32 = SM.bitcast(F32)
        A3d = A[:, :].rearrange("p (c f) -> p c f", f=2048)
        B3d = B[:, :].rearrange("p (c f) -> p c f", f=2048)
        C3dP = C[:, :].rearrange("p (c f) -> p c f", f=512)

        IDC = 4096
        MKBC = 4224
        DGC = 4480
        SEC32, RC32 = 2368, 2384
        ONC = 4800
        BOC = 4928
        WTC32 = 3488
        WRC = 7232
        LGC32, EXC32 = 3680, 3688
        E2C32, DEN32, M132 = 3696, 3697, 3698
        SELC32 = 3700
        S_SUM, S_SQ, S_MEAN, S_VAR, S_STD, S_RSTD, S_NMEAN, S_T1 = (
            3710, 3711, 3712, 3713, 3714, 3715, 3716, 3717)
        XSLOT = [0, 2048, 12288, 14336]
        XR32 = [0, 2048]
        WR32 = [4096, 6144]
        ACRD = [0, 2048]
        ACWR = [8192, 10240]
        WGB = [16384, 18432]
        WUB = [20480, 22528]
        QW = [12288, 14336]

        def gdma(ctr, dst, src, waits=()):
            return sch.emit("gpsimd", lambda g, d=dst, s=src: g.dma_start(out=d, in_=s),
                            waits, ctr, 16)

        def tslot(g):
            return (PA, g * 512) if g < 2 else (PB, (g - 2) * 512)

        m_xld = {}
        m_xt = {}
        m_xevac = {}
        m_v = {}
        m_vevac = {}
        last_tsl = {0: 0, 1: 0, 2: 0, 3: 0}

        m_xld[0] = gdma(XLD[0], C[:, XSLOT[0]:XSLOT[0] + 2048], x_h[0:128, :])
        m_xld[1] = gdma(XLD[1], C[:, XSLOT[1]:XSLOT[1] + 2048], x_h[128:256, :])
        gdma(CONST, SM[:, IDC:IDC + 128], id_h[:, :])
        gdma(CONST, SM[:, MKBC:MKBC + 128], mk_h[:, :])
        m_xld[2] = gdma(XLD[2], C[:, XSLOT[2]:XSLOT[2] + 2048], x_h[256:384, :])
        m_xld[3] = gdma(XLD[3], C[:, XSLOT[3]:XSLOT[3] + 2048], x_h[384:512, :])
        gdma(CONST, SM[0:1, ONC:ONC + 128], on_h[:, :])
        gdma(CONST, SM[0:1, BOC:BOC + 2048], bo_h[:, :])
        gdma(CONST, SM[:, WRC:WRC + 128], wr_h[:, :])
        gdma(CONST, B[:, 0:8192], wv_h[:, :])
        gdma(CONST, C[:, 4096:12288], wk_h[:, :])
        m_const = CONST.n

        for tt in range(NT):
            if tt >= 4:
                m_xld[tt] = gdma(XLD[tt % 4], C[:, XSLOT[tt % 4]:XSLOT[tt % 4] + 2048],
                                 x_h[tt * 128:(tt + 1) * 128, :],
                                 [(PE, m_xt[(tt - 4, 3)])])
            for g in range(4):
                ps, off = tslot(g)
                def mk_tr(t_, tt=tt, g=g, ps=ps, off=off):
                    ins = None
                    for j in range(4):
                        dc = g * 4 + j
                        ins = t_.matmul(ps[:, off + j * 128: off + (j + 1) * 128],
                                        C[:, XSLOT[tt % 4] + dc * 128:
                                          XSLOT[tt % 4] + (dc + 1) * 128],
                                        SM[:, IDC:IDC + 128], start=True, stop=True)
                    return ins
                m_xt[(tt, g)] = sch.emit("tensor", mk_tr,
                                         [(XLD[tt % 4], m_xld[tt]), (CONST, m_const),
                                          (VE, last_tsl[g])], PE)
                def mk_ev(v, tt=tt, g=g, ps=ps, off=off):
                    dst = A3d[:, g * 4:(g + 1) * 4, tt * 128:(tt + 1) * 128]
                    src = ps[:, off:off + 512].rearrange("p (c f) -> p c f", f=128)
                    return v.tensor_copy(dst, src)
                m_xevac[(tt, g)] = sch.emit("vector", mk_ev, [(PE, m_xt[(tt, g)])], VE)
                last_tsl[g] = m_xevac[(tt, g)]
            def mk_v(t_, tt=tt):
                ins = None
                for dc in range(ND):
                    ins = t_.matmul(PA[:, 1024:1536],
                                    A3d[:, dc, tt * 128:(tt + 1) * 128],
                                    B[:, dc * 512:(dc + 1) * 512],
                                    start=(dc == 0), stop=(dc == ND - 1))
                return ins
            m_v[tt] = sch.emit("tensor", mk_v,
                               [(VE, m_xevac[(tt, 3)]), (CONST, m_const),
                                (VE, m_vevac.get(tt - 1, 0))], PE)
            m_vevac[tt] = sch.emit(
                "vector",
                lambda v, tt=tt: v.tensor_copy(
                    Dd[:, 8192 + tt * 512: 8192 + (tt + 1) * 512], PA[:, 1024:1536]),
                [(PE, m_v[tt])], VE)

        prev_k = 0
        for g in range(NG):
            for ts in range(4):
                def mk_k(t_, g=g, ts=ts):
                    ins = None
                    for dc in range(ND):
                        ins = t_.matmul(PA[:, 1536:2048],
                                        C[:, 4096 + dc * 512 + g * 128:
                                          4096 + dc * 512 + (g + 1) * 128],
                                        A3d[:, dc, ts * 512:(ts + 1) * 512],
                                        start=(dc == 0), stop=(dc == ND - 1))
                    return ins
                m_k = sch.emit("tensor", mk_k,
                               [(CONST, m_const), (VE, m_xevac[(NT - 1, 3)]),
                                (VE, prev_k)], PE)
                prev_k = sch.emit(
                    "vector",
                    lambda v, g=g, ts=ts: v.tensor_copy(
                        Dd[:, g * 2048 + ts * 512: g * 2048 + (ts + 1) * 512],
                        PA[:, 1536:2048]),
                    [(PE, m_k)], VE)
        m_k_done = prev_k

        m_wq = {}
        m_q = {}
        m_qevac = {}
        prev_q = {0: 0, 1: 0}
        for h in range(n_heads):
            wqw = [(PE, m_q.get(h - 2, 0))]
            if h < 2:
                wqw.append((PE, m_xt[(NT - 2 + h, 3)]))
            m_wq[h] = gdma(WQL[h % 2], C[:, QW[h % 2]:QW[h % 2] + 2048],
                           wq_h[:, h * 2048:(h + 1) * 2048], wqw)
            for ts in range(4):
                po = 1024 + (ts % 2) * 512
                def mk_q(t_, h=h, ts=ts, po=po):
                    ins = None
                    for dc in range(ND):
                        ins = t_.matmul(PB[:, po:po + 512],
                                        C[:, QW[h % 2] + dc * 128:
                                          QW[h % 2] + (dc + 1) * 128],
                                        A3d[:, dc, ts * 512:(ts + 1) * 512],
                                        start=(dc == 0), stop=(dc == ND - 1))
                    return ins
                waits = [(WQL[h % 2], m_wq[h]), (VE, prev_q[ts % 2])]
                if h == 0 and ts == 0:
                    waits.append((PE, m_v[NT - 1]))
                m_q[h] = sch.emit("tensor", mk_q, waits, PE)
                m_qevac[(h, ts)] = sch.emit(
                    "vector",
                    lambda v, h=h, ts=ts, po=po: v.tensor_scalar(
                        B[:, h * 2048 + ts * 512: h * 2048 + (ts + 1) * 512],
                        PB[:, po:po + 512], QSCALE, None, ALU.mult),
                    [(PE, m_q[h])], VE)
                prev_q[ts % 2] = m_qevac[(h, ts)]
        m_q_all = m_q.get(n_heads - 1, 0)

        m_tapA = 0
        if dbg:
            m_tapA = sch.emit(
                "sync", lambda sy: sy.dma_start(out=tA_h[:, :], in_=A[:, :]),
                [(VE, m_xevac[(NT - 1, 3)]), (PE, m_q_all)], STT[0], 16)
        for dc in range(ND):
            w0_ = [(PE, m_q_all)] + ([(STT[0], m_tapA)] if dbg else [])
            gdma(WOL, A3d[:, dc, :], wo_h[dc * 128:(dc + 1) * 128, :],
                 w0_ if dc == 0 else ())
        m_wo_all = WOL.n

        m_exp = {}
        last_exp = 0
        last_probs_rd = {0: 0, 1: 0}
        t3free = {0: 0, 1: 0, 2: 0}
        m_ctx = 0
        m_ctx_evac = 0
        m_recip = {}
        m_diag = {}
        m_scmb = {}
        m_bank = {}
        m_pm_d = {}
        SEC_H = [SEC32, 3744]
        RC_H = [RC32, 3760]
        M1_H = [M132, 3699]
        PB_H = [3724, 3780]
        PRB_H = [0, 2048]
        PTB_H = [0, 16]
        DG_H = [DGC, DGC + 128]

        def emit_scores_mm(h, tt):
            g = h // 4
            hp = h % 2
            kw = (tt + 1) * 128
            nch = (kw + 511) // 512
            for ns in range(nch):
                w_ = min(512, kw - ns * 512)
                last = (ns == nch - 1)
                def mk_sc(t_, h=h, tt=tt, g=g, ns=ns, w_=w_, last=last, kw=kw):
                    ins = t_.matmul(
                        PA[:, ns * 512: ns * 512 + w_],
                        B[:, h * 2048 + tt * 128: h * 2048 + (tt + 1) * 128],
                        Dd[:, g * 2048 + ns * 512: g * 2048 + ns * 512 + w_],
                        start=True, stop=not last)
                    if last:
                        ins = t_.matmul(PA[:, kw - 128:kw], SM[:, IDC:IDC + 128],
                                        SM[:, MKBC:MKBC + 128], start=False, stop=True)
                    return ins
                w0 = [(AC, m_bank.get(ns, 0))]
                if ns == 0:
                    w0 += [(VE, m_qevac[(h, 3)]), (VE, m_k_done), (CONST, m_const)]
                m_sc = sch.emit("tensor", mk_sc, w0, PE)
            m_pm_d[(hp, tt)] = m_sc

        m_mx_d = {}

        def emit_max(h, tt):
            hp = h % 2
            kw = (tt + 1) * 128
            m_mx_d[hp] = sch.emit("vector",
                                  lambda v, kw=kw, hp=hp: v.reduce_max(
                                      SM32[:, M1_H[hp]:M1_H[hp] + 1], PA[:, 0:kw],
                                      axis=AX.X, negate=True),
                                  [(PE, m_pm_d[(hp, tt)])], VE)

        def emit_exp(h, tt):
            hp = h % 2
            kw = (tt + 1) * 128
            nch = (kw + 511) // 512
            m_mx = m_mx_d[hp]
            m_e = 0
            for ns in range(nch):
                w_ = min(512, kw - ns * 512)
                ew = [(VE, m_mx)]
                if ns == 0:
                    ew += [(PE, last_probs_rd[hp]), (VE, m_scmb.get(hp, 0))]
                m_e = sch.emit(
                    "scalar",
                    lambda s, tt=tt, ns=ns, w_=w_, hp=hp: s.activation(
                        SM[:, PRB_H[hp] + ns * 512: PRB_H[hp] + ns * 512 + w_],
                        PA[:, ns * 512: ns * 512 + w_],
                        AF.Exp, bias=SM32[:, M1_H[hp]:M1_H[hp] + 1],
                        accum_out=SM32[:, PB_H[hp] + ns:PB_H[hp] + ns + 1]),
                    ew, AC)
                m_bank[ns] = m_e
            m_exp[hp] = m_e
            m_scmb[hp] = sch.emit(
                "vector",
                lambda v, tt=tt, nch=nch, hp=hp: v.reduce_sum(
                    SM32[:, SEC_H[hp] + tt:SEC_H[hp] + tt + 1],
                    SM32[:, PB_H[hp]:PB_H[hp] + nch], axis=AX.X),
                [(AC, m_e)], VE)
            m_recip[hp] = sch.emit(
                "vector",
                lambda v, tt=tt, hp=hp: v.reciprocal(
                    SM32[:, RC_H[hp] + tt:RC_H[hp] + tt + 1],
                    SM32[:, SEC_H[hp] + tt:SEC_H[hp] + tt + 1]),
                [(VE, m_scmb[hp])], VE)
            m_diag[hp] = sch.emit(
                "scalar",
                lambda s, tt=tt, hp=hp: s.activation(
                    SM[:, DG_H[hp]:DG_H[hp] + 128],
                    SM[:, IDC:IDC + 128], AF.Identity,
                    scale=SM32[:, RC_H[hp] + tt:RC_H[hp] + tt + 1]),
                [(VE, m_recip[hp])], AC)
            return m_e

        def emit_transposes(h, tt):
            hp = h % 2
            lastpe = 0
            ngr = (tt + 4) // 4
            for gr in range(ngr):
                sl = gr % 3
                k0 = gr * 4
                kn = min(4, tt + 1 - k0)
                j = tt % 4
                def mk_tp(t_, tt=tt, k0=k0, kn=kn, sl=sl, hp=hp):
                    ins = None
                    for jj in range(kn):
                        kt = k0 + jj
                        ins = t_.matmul(
                            PB[:, sl * 512 + jj * 128: sl * 512 + (jj + 1) * 128],
                            SM[:, PRB_H[hp] + kt * 128: PRB_H[hp] + (kt + 1) * 128],
                            SM[:, DG_H[hp]:DG_H[hp] + 128],
                            start=True, stop=True)
                    return ins
                m_tg = sch.emit("tensor", mk_tp,
                                [(AC, m_diag[hp]), (VE, t3free[sl])], PE)
                lastpe = m_tg
                def mk_te(v, k0=k0, kn=kn, sl=sl, j=j, hp=hp):
                    dst = C3dP[:, PTB_H[hp] + k0: PTB_H[hp] + k0 + kn,
                               j * 128:(j + 1) * 128]
                    src = PB[:, sl * 512: sl * 512 + kn * 128].rearrange(
                        "p (c f) -> p c f", f=128)
                    return v.tensor_copy(dst, src)
                t3free[sl] = sch.emit("vector", mk_te, [(PE, m_tg)], VE)
            last_probs_rd[hp] = lastpe

        def emit_ctx(h, qb):
            nonlocal m_ctx, m_ctx_evac
            g = h // 4
            hp = h % 2
            nkt = qb * 4 + 4
            def mk_cx(t_, g=g, qb=qb, nkt=nkt, hp=hp):
                ins = None
                for kt in range(nkt):
                    off = max(0, kt - qb * 4) * 128
                    ins = t_.matmul(
                        PB[:, 1536 + off:2048],
                        Dd[:, 8192 + kt * 512 + g * 128:
                           8192 + kt * 512 + (g + 1) * 128],
                        C[:, (PTB_H[hp] + kt) * 512 + off:
                          (PTB_H[hp] + kt) * 512 + 512],
                        start=(kt == 0), stop=(kt == nkt - 1))
                return ins
            m_ctx = sch.emit("tensor", mk_cx,
                             [(VE, max(t3free.values())), (VE, m_vevac[NT - 1]),
                              (VE, m_ctx_evac)], PE)
            m_ctx_evac = sch.emit(
                "vector",
                lambda v, h=h, qb=qb: v.tensor_copy(
                    B[:, h * 2048 + qb * 512: h * 2048 + (qb + 1) * 512],
                    PB[:, 1536:2048]),
                [(PE, m_ctx)], VE)

        for hA in range(0, n_heads, 2):
            hB = hA + 1
            for tt in range(NT):
                emit_scores_mm(hA, tt)
                emit_max(hA, tt)
                if tt >= 1:
                    emit_transposes(hB, tt - 1)
                    if (tt - 1) % 4 == 3:
                        emit_ctx(hB, (tt - 1) // 4)
                last_exp = emit_exp(hA, tt)
                emit_scores_mm(hB, tt)
                emit_max(hB, tt)
                emit_transposes(hA, tt)
                if tt % 4 == 3:
                    emit_ctx(hA, tt // 4)
                last_exp = emit_exp(hB, tt)
            emit_transposes(hB, NT - 1)
            emit_ctx(hB, 3)
        m_ctx_all = m_ctx_evac

        m_xres = {}
        m_y1st = {}
        m_y1tok_d = {}
        m_wo_mm = {}
        m_stt = {}
        m_sq = {}
        m_y1evac = {}
        m_yt_tile = {}
        last_y1sl = {0: 0, 1: 0, 2: 0, 3: 0}

        def emit_ln1(waits):
            e1 = sch.emit("vector", lambda v: (
                v.tensor_scalar(SM32[:, S_MEAN:S_MEAN + 1], SM32[:, S_SUM:S_SUM + 1],
                                1.0 / D, None, ALU.mult),
                v.tensor_scalar(SM32[:, S_T1:S_T1 + 1], SM32[:, S_SQ:S_SQ + 1],
                                1.0 / D, None, ALU.mult))[-1], waits, VE)
            e2 = sch.emit("vector", lambda v: v.tensor_tensor(
                SM32[:, S_VAR:S_VAR + 1], SM32[:, S_MEAN:S_MEAN + 1],
                SM32[:, S_MEAN:S_MEAN + 1], ALU.mult), [(VE, e1)], VE)
            e3 = sch.emit("vector", lambda v: v.tensor_tensor(
                SM32[:, S_T1:S_T1 + 1], SM32[:, S_T1:S_T1 + 1],
                SM32[:, S_VAR:S_VAR + 1], ALU.subtract), [(VE, e2)], VE)
            return sch.emit("vector", lambda v: v.tensor_scalar(
                SM32[:, S_T1:S_T1 + 1], SM32[:, S_T1:S_T1 + 1],
                EPS, None, ALU.add), [(VE, e3)], VE)

        def mk_sqrt(s):
            return s.activation(SM32[:, S_STD:S_STD + 1], SM32[:, S_T1:S_T1 + 1],
                                AF.Sqrt)

        for tt in range(NT):
            m_xres[tt] = sch.emit(
                "sync",
                lambda sy, tt=tt: sy.dma_start(
                    out=Dd32[:, XR32[tt % 2]:XR32[tt % 2] + 2048],
                    in_=x_h[tt * 128:(tt + 1) * 128, :]),
                [(VE, m_stt.get(tt - 2, 0))] + ([(VE, m_ctx_all)] if tt < 2 else [])
                + ([(STT[1], m_tapD)] if (dbg and tt < 2) else []),
                SYL[tt % 2], 16)
            def mk_wo(t_, tt=tt):
                ins = None
                for dc in range(n_heads):
                    for nb in range(4):
                        ins = t_.matmul(PA[:, nb * 512:(nb + 1) * 512],
                                        B3d[:, dc, tt * 128:(tt + 1) * 128],
                                        A3d[:, dc, nb * 512:(nb + 1) * 512],
                                        start=(dc == 0), stop=False)
                for nb in range(4):
                    ins = t_.matmul(PA[:, nb * 512:(nb + 1) * 512],
                                    SM[0:1, ONC:ONC + 128],
                                    SM[0:1, BOC + nb * 512: BOC + (nb + 1) * 512],
                                    start=False, stop=True)
                return ins
            waits = [(VE, m_ctx_all), (WOL, m_wo_all), (CONST, m_const),
                     (VE, m_stt.get(tt - 1, 0))]
            if tt == 0:
                waits.append((AC, last_exp))
            m_wo_mm[tt] = sch.emit("tensor", mk_wo, waits, PE)
            m_stt[tt] = sch.emit(
                "vector",
                lambda v, tt=tt: v.scalar_tensor_tensor(
                    Dd32[:, WR32[0]:WR32[0] + 2048], PA[:, 0:2048], 1.0,
                    Dd32[:, XR32[tt % 2]:XR32[tt % 2] + 2048], ALU.mult, ALU.add,
                    accum_out=SM32[:, S_SUM:S_SUM + 1]),
                [(PE, m_wo_mm[tt]), (SYL[tt % 2], m_xres[tt])], VE)
            m_sq[tt] = sch.emit(
                "scalar",
                lambda s: s.activation(Dd32[:, WR32[1]:WR32[1] + 2048],
                                       Dd32[:, WR32[0]:WR32[0] + 2048],
                                       AF.Square, accum_out=SM32[:, S_SQ:S_SQ + 1]),
                [(VE, m_stt[tt])], AC)
            m_v1 = emit_ln1([(AC, m_sq[tt])])
            m_sd = sch.emit("scalar", mk_sqrt, [(VE, m_v1)], AC)
            def mk_ln2a(v):
                v.reciprocal(SM32[:, S_RSTD:S_RSTD + 1], SM32[:, S_STD:S_STD + 1])
                return v.tensor_scalar(SM32[:, S_NMEAN:S_NMEAN + 1],
                                       SM32[:, S_MEAN:S_MEAN + 1], -1.0, None, ALU.mult)

            def mk_ln2b(v):
                return v.tensor_tensor(SM32[:, S_NMEAN:S_NMEAN + 1],
                                       SM32[:, S_NMEAN:S_NMEAN + 1],
                                       SM32[:, S_RSTD:S_RSTD + 1], ALU.mult)
            m_ln2a = sch.emit("vector", mk_ln2a, [(AC, m_sd)], VE)
            m_ln2v = sch.emit("vector", mk_ln2b, [(VE, m_ln2a)], VE)
            m_y1tok_d[tt] = m_y1tok = sch.emit(
                "scalar",
                lambda s, tt=tt: s.activation(
                    C[:, XSLOT[tt % 2]:XSLOT[tt % 2] + 2048],
                    Dd32[:, WR32[0]:WR32[0] + 2048], AF.Identity,
                    scale=SM32[:, S_RSTD:S_RSTD + 1],
                    bias=SM32[:, S_NMEAN:S_NMEAN + 1]),
                [(VE, m_ln2v), (PE, m_yt_tile.get(tt - 2, 0)),
                 (SYL[tt % 2], m_y1st.get(tt - 2, 0))], AC)
            m_y1st[tt] = sch.emit(
                "sync",
                lambda sy, tt=tt: sy.dma_start(
                    out=y1tok_h[tt * 128:(tt + 1) * 128, :],
                    in_=C[:, XSLOT[tt % 2]:XSLOT[tt % 2] + 2048]),
                [(AC, m_y1tok)], SYL[tt % 2], 16)
            def emit_y1t(tt2):
                for g4 in range(4):
                    ps, off = PB, g4 * 512
                    def mk_yt(t_, tt2=tt2, g4=g4, ps=ps, off=off):
                        ins = None
                        for jj in range(4):
                            dc = g4 * 4 + jj
                            ins = t_.matmul(ps[:, off + jj * 128: off + (jj + 1) * 128],
                                            C[:, XSLOT[tt2 % 2] + dc * 128:
                                              XSLOT[tt2 % 2] + (dc + 1) * 128],
                                            SM[:, IDC:IDC + 128], start=True, stop=True)
                        return ins
                    m_yt = sch.emit("tensor", mk_yt,
                                    [(AC, m_y1tok_d[tt2]), (VE, last_y1sl[g4])], PE)
                    def mk_ye(v, tt2=tt2, g4=g4, ps=ps, off=off):
                        dst = B3d[:, g4 * 4:(g4 + 1) * 4, tt2 * 128:(tt2 + 1) * 128]
                        src = ps[:, off:off + 512].rearrange("p (c f) -> p c f", f=128)
                        return v.tensor_copy(dst, src)
                    m_y1evac[(tt2, g4)] = sch.emit("vector", mk_ye, [(PE, m_yt)], VE)
                    last_y1sl[g4] = m_y1evac[(tt2, g4)]
                m_yt_tile[tt2] = m_yt
            if tt >= 1:
                emit_y1t(tt - 1)
            if tt == NT - 1:
                emit_y1t(NT - 1)
        m_y1_done = m_y1evac[(NT - 1, 3)]
        m_yt_last = m_yt_tile[NT - 1]

        m_w = {}
        prev_r1 = 0
        for tt in range(NT):
            def mk_rl(t_, tt=tt):
                ins = None
                for dc in range(ND):
                    ins = t_.matmul(PB[:, 0:8],
                                    B3d[:, dc, tt * 128:(tt + 1) * 128],
                                    SM[:, WRC + dc * 8: WRC + (dc + 1) * 8],
                                    start=(dc == 0), stop=(dc == ND - 1))
                return ins
            rlw = [(VE, m_y1evac[(tt, 3)]), (CONST, m_const), (AC, prev_r1)]
            if tt == 0:
                rlw.append((VE, m_y1_done))
            m_rl = sch.emit("tensor", mk_rl, rlw, PE)
            m_r1 = sch.emit("vector",
                            lambda v: v.reduce_max(SM32[:, M132:M132 + 1], PB[:, 0:8],
                                                   axis=AX.X, negate=True),
                            [(PE, m_rl)], VE)
            m_re = sch.emit("scalar",
                            lambda s: s.activation(SM32[:, EXC32:EXC32 + 8],
                                                   PB[:, 0:8], AF.Exp,
                                                   bias=SM32[:, M132:M132 + 1]),
                            [(VE, m_r1)], AC)
            prev_r1 = m_re
            m_0 = sch.emit("vector", lambda v: v.tensor_scalar(
                SM32[:, SELC32:SELC32 + 8], SM32[:, EXC32:EXC32 + 8],
                1.0, None, ALU.is_lt), [(AC, m_re)], VE)
            m_1 = sch.emit("vector", lambda v: v.tensor_tensor(
                SM32[:, SELC32:SELC32 + 8], SM32[:, SELC32:SELC32 + 8],
                SM32[:, EXC32:EXC32 + 8], ALU.mult), [(VE, m_0)], VE)
            m_2 = sch.emit("vector", lambda v: v.reduce_max(
                SM32[:, E2C32:E2C32 + 1], SM32[:, SELC32:SELC32 + 8],
                axis=AX.X), [(VE, m_1)], VE)
            m_3 = sch.emit("vector", lambda v: v.tensor_scalar(
                SM32[:, DEN32:DEN32 + 1], SM32[:, E2C32:E2C32 + 1],
                1.0, None, ALU.add), [(VE, m_2)], VE)
            m_4 = sch.emit("vector", lambda v: v.reciprocal(
                SM32[:, DEN32:DEN32 + 1], SM32[:, DEN32:DEN32 + 1]), [(VE, m_3)], VE)
            m_5 = sch.emit("vector", lambda v: v.tensor_scalar(
                SM32[:, SELC32:SELC32 + 8], SM32[:, EXC32:EXC32 + 8],
                SM32[:, E2C32:E2C32 + 1], None, ALU.is_ge), [(VE, m_4)], VE)
            m_6 = sch.emit("vector", lambda v: v.tensor_tensor(
                SM32[:, SELC32:SELC32 + 8], SM32[:, SELC32:SELC32 + 8],
                SM32[:, EXC32:EXC32 + 8], ALU.mult), [(VE, m_5)], VE)
            m_w[tt] = sch.emit("vector", lambda v, tt=tt: v.tensor_scalar(
                SM32[:, WTC32 + tt * 8: WTC32 + (tt + 1) * 8],
                SM32[:, SELC32:SELC32 + 8],
                SM32[:, DEN32:DEN32 + 1], None, ALU.mult), [(VE, m_6)], VE)
        m_w_done = m_w[NT - 1]
        m_tapC = 0
        if dbg:
            sch.emit("sync", lambda sy: sy.dma_start(out=tB2_h[:, :], in_=B[:, :]),
                     [(VE, m_y1_done)], STT[3], 16)
            sch.emit("sync", lambda sy: sy.dma_start(out=tS1_h[:, :],
                                                     in_=Dd32[:, 4096:8192]),
                     [(VE, m_y1_done)], STT[5], 16)
            m_tapC = sch.emit("sync",
                              lambda sy: sy.dma_start(out=tC_h[:, :], in_=C[:, 0:4096]),
                              [(AC, m_y1tok)], STT[6], 16)
            sch.emit("sync", lambda sy: sy.dma_start(out=tST_h[:, :],
                                                     in_=SM32[:, 3710:3718]),
                     [(VE, m_y1_done)], STT[7], 16)
            sch.emit("sync", lambda sy: sy.dma_start(out=tW_h[:, :], in_=SM32[:, WTC32:WTC32 + 128]),
                     [(VE, m_w_done)], STT[4], 16)

        m_gu = {}
        m_silu = {}
        m_hm = {}
        m_eb = {}
        m_accld = {}
        m_bstt = {}
        m_accst = {}
        m_wdld = {}
        m_f1add = {}
        m_y2st = {}
        m_eb_last = 0

        def prev_key(e, fh, tt):
            k = tt - 2
            if k >= 0:
                return (e, fh, k)
            if fh == 1:
                return (e, 0, NT + k)
            if e > 0:
                return (e - 1, 1, NT + k)
            return None

        for e in range(n_experts):
            for fh in range(2):
                half0 = (e == 0 and fh == 0)
                is_last = (e == n_experts - 1 and fh == 1)
                for lfc in range(8):
                    fc = fh * 8 + lfc
                    if fc >= 2:
                        wgw = [(PE, m_gu[(e, fc - 2, 1)])]
                    elif e > 0:
                        wgw = [(PE, m_gu[(e - 1, 14 + fc, 1)])]
                    else:
                        wgw = [(PE, m_wo_mm[NT - 1])]
                    m_wgld = gdma(WGL[fc % 2], A[:, WGB[fc % 2]:WGB[fc % 2] + 2048],
                                  wg_h[e, fc, :, :], wgw)
                    m_wuld = gdma(WUL[fc % 2], A[:, WUB[fc % 2]:WUB[fc % 2] + 2048],
                                  wu_h[e, fc, :, :])
                    if lfc >= 2:
                        lw = lfc - 2
                        wdw = [(PE, m_eb_last)]
                        if half0:
                            wdw.append((PE, m_yt_last))
                            if lw < 2:
                                wdw.append((SYL[lw % 2], m_y1st[14 + (lw % 2)]))
                            if dbg:
                                wdw.append((STT[6], m_tapC))
                        m_wdld[(e, fh, lw)] = gdma(
                            WDL, C[:, lw * 2048:(lw + 1) * 2048],
                            wd_h[e, (fh * 8 + lw) * 128:(fh * 8 + lw + 1) * 128, :], wdw)
                    for th in range(2):
                        ps = PA if th == 0 else PB
                        def mk_gu(t_, fc=fc, th=th, ps=ps):
                            ins = None
                            for dc in range(ND):
                                for s2 in range(2):
                                    c0 = th * 1024 + s2 * 512
                                    ins = t_.matmul(
                                        ps[:, s2 * 512:(s2 + 1) * 512],
                                        A[:, WGB[fc % 2] + dc * 128:
                                          WGB[fc % 2] + (dc + 1) * 128],
                                        B3d[:, dc, c0:c0 + 512],
                                        start=(dc == 0), stop=(dc == ND - 1))
                                for s2 in range(2):
                                    c0 = th * 1024 + s2 * 512
                                    ins = t_.matmul(
                                        ps[:, 1024 + s2 * 512: 1024 + (s2 + 1) * 512],
                                        A[:, WUB[fc % 2] + dc * 128:
                                          WUB[fc % 2] + (dc + 1) * 128],
                                        B3d[:, dc, c0:c0 + 512],
                                        start=(dc == 0), stop=(dc == ND - 1))
                            return ins
                        waits = [(WGL[fc % 2], m_wgld), (WUL[fc % 2], m_wuld),
                                 (AC, m_silu.get((e, fc - 1, th), 0)),
                                 (VE, m_hm.get((e, fc - 1, th), 0))]
                        if lfc == 0:
                            waits.append((PE, m_eb_last))
                            ptt = NT - 2 + th
                            prev_b = m_bstt.get((e, fh - 1, ptt),
                                                m_bstt.get((e - 1, 1, ptt), 0))
                            waits.append((VE, prev_b))
                            if half0:
                                waits.append((VE, m_w_done))
                        m_gu[(e, fc, th)] = sch.emit("tensor", mk_gu, waits, PE)
                        m_silu[(e, fc, th)] = sch.emit(
                            "scalar",
                            lambda s, th=th, ps=ps: s.activation(
                                SM32[:, th * 1024:(th + 1) * 1024], ps[:, 0:1024],
                                AF.Silu),
                            [(PE, m_gu[(e, fc, th)]),
                             (VE, m_hm.get((e, fc - 1, th), 0))], AC)
                        m_hm[(e, fc, th)] = sch.emit(
                            "vector",
                            lambda v, fc=fc, th=th, ps=ps: v.tensor_tensor(
                                A[:, (fc % 8) * 2048 + th * 1024:
                                  (fc % 8) * 2048 + (th + 1) * 1024],
                                SM32[:, th * 1024:(th + 1) * 1024],
                                ps[:, 1024:2048], ALU.mult),
                            [(AC, m_silu[(e, fc, th)])], VE)
                for lw in (6, 7):
                    wdw = [(PE, m_eb_last)]
                    if half0:
                        wdw.append((PE, m_yt_last))
                        if dbg:
                            wdw.append((STT[6], m_tapC))
                    m_wdld[(e, fh, lw)] = gdma(
                        WDL, C[:, lw * 2048:(lw + 1) * 2048],
                        wd_h[e, (fh * 8 + lw) * 128:(fh * 8 + lw + 1) * 128, :], wdw)
                for tt in range(NT):
                    eps_ = PA if tt % 2 == 0 else PB
                    ops_ = PB if tt % 2 == 0 else PA
                    def mk_eb(t_, tt=tt, eps_=eps_):
                        ins = None
                        for lfc in range(8):
                            for nb in range(4):
                                ins = t_.matmul(
                                    eps_[:, nb * 512:(nb + 1) * 512],
                                    A[:, lfc * 2048 + tt * 128:
                                      lfc * 2048 + (tt + 1) * 128],
                                    C[:, lfc * 2048 + nb * 512:
                                      lfc * 2048 + (nb + 1) * 512],
                                    start=(lfc == 0), stop=(lfc == 7))
                        return ins
                    waits = [(VE, m_hm[(e, fh * 8 + 7, 1)]),
                             (WDL, m_wdld[(e, fh, 7)]),
                             (VE, m_bstt.get((e, fh, tt - 2), 0))]
                    if tt < 2:
                        th = tt % 2
                        waits += [(AC, m_silu[(e, fh * 8 + 7, th)]),
                                  (VE, m_hm[(e, fh * 8 + 7, th)])]
                    m_eb[(e, fh, tt)] = sch.emit("tensor", mk_eb, waits, PE)
                    first = half0
                    pk = prev_key(e, fh, tt)
                    if not first:
                        ldw = [(VE, m_bstt.get(pk, 0) if pk else 0),
                               (STT[tt], m_accst.get((e, fh - 1, tt),
                                    m_accst.get((e - 1, 1, tt), 0)))]
                        if is_last:
                            ldw.append((STT[(tt - 2) % NT], m_y2st.get(tt - 2, 0)))
                        m_accld[(e, fh, tt)] = sch.emit(
                            "sync",
                            lambda sy, tt=tt: sy.dma_start(
                                out=Dd[:, ACRD[tt % 2]:ACRD[tt % 2] + 2048],
                                in_=acc_h[tt * 128:(tt + 1) * 128, :]),
                            ldw, SYL[tt % 2], 16)
                    wc0 = WTC32 + tt * 8 + e
                    if first:
                        def mk_st(v, tt=tt, eps_=eps_, wc0=wc0):
                            return v.tensor_scalar(
                                Dd[:, ACWR[tt % 2]:ACWR[tt % 2] + 2048],
                                eps_[:, 0:2048], SM32[:, wc0:wc0 + 1], None, ALU.mult)
                        stw = [(PE, m_eb[(e, fh, tt)]),
                               (STT[(tt - 2) % NT],
                                m_accst.get(pk, 0) if pk else 0),
                               (VE, m_w[tt])]
                        if tt < 2:
                            stw.append((AC, m_sq[NT - 1]))
                            if dbg:
                                stw.append((STT[5], 16))
                        m_bstt[(e, fh, tt)] = sch.emit("vector", mk_st, stw, VE)
                    else:
                        def mk_st(v, tt=tt, eps_=eps_, wc0=wc0, is_last=is_last):
                            out_ap = (Dd32[:, WR32[tt % 2]:WR32[tt % 2] + 2048]
                                      if is_last else
                                      Dd[:, ACWR[tt % 2]:ACWR[tt % 2] + 2048])
                            return v.scalar_tensor_tensor(
                                out_ap,
                                eps_[:, 0:2048], SM32[:, wc0:wc0 + 1],
                                Dd[:, ACRD[tt % 2]:ACRD[tt % 2] + 2048],
                                ALU.mult, ALU.add)
                        stw = [(PE, m_eb[(e, fh, tt)]),
                               (SYL[tt % 2], m_accld[(e, fh, tt)]),
                               (STT[(tt - 2) % NT],
                                m_accst.get(pk, 0) if pk else 0),
                               (VE, m_w[tt])]
                        if is_last:
                            stw.append((STT[(tt - 2) % NT], m_y2st.get(tt - 2, 0)))
                        m_bstt[(e, fh, tt)] = sch.emit("vector", mk_st, stw, VE)
                    if not is_last:
                        m_accst[(e, fh, tt)] = sch.emit(
                            "sync",
                            lambda sy, tt=tt: sy.dma_start(
                                out=acc_h[tt * 128:(tt + 1) * 128, :],
                                in_=Dd[:, ACWR[tt % 2]:ACWR[tt % 2] + 2048]),
                            [(VE, m_bstt[(e, fh, tt)])], STT[tt], 16)
                    else:
                        Y1SL = [24576, 26624]
                        m_y1ld = sch.emit(
                            "sync",
                            lambda sy, tt=tt: sy.dma_start(
                                out=A[:, Y1SL[tt % 2]:Y1SL[tt % 2] + 2048],
                                in_=y1tok_h[tt * 128:(tt + 1) * 128, :]),
                            [(SYL[tt % 2], m_y1st[tt]),
                             (VE, m_f1add.get(tt - 2, 0))], SYL[tt % 2], 16)
                        def mk_fa(v, tt=tt):
                            return v.scalar_tensor_tensor(
                                Dd32[:, WR32[tt % 2]:WR32[tt % 2] + 2048],
                                Dd32[:, WR32[tt % 2]:WR32[tt % 2] + 2048], 1.0,
                                A[:, Y1SL[tt % 2]:Y1SL[tt % 2] + 2048],
                                ALU.mult, ALU.add,
                                accum_out=SM32[:, S_SUM:S_SUM + 1])
                        m_f1add[tt] = sch.emit(
                            "vector", mk_fa,
                            [(VE, m_bstt[(e, fh, tt)]),
                             (SYL[tt % 2], m_y1ld)], VE)
                        m_fs = m_f1add[tt]
                        m_fsq = sch.emit(
                            "scalar",
                            lambda s, tt=tt: s.activation(
                                Dd32[:, XR32[tt % 2]:XR32[tt % 2] + 2048],
                                Dd32[:, WR32[tt % 2]:WR32[tt % 2] + 2048],
                                AF.Square, accum_out=SM32[:, S_SQ:S_SQ + 1]),
                            [(VE, m_fs)], AC)
                        m_fv1 = emit_ln1([(AC, m_fsq)])
                        m_fsd = sch.emit("scalar", mk_sqrt, [(VE, m_fv1)], AC)
                        m_flna = sch.emit("vector", mk_ln2a, [(AC, m_fsd)], VE)
                        m_flnv = sch.emit("vector", mk_ln2b, [(VE, m_flna)], VE)
                        m_fy2 = sch.emit(
                            "scalar",
                            lambda s, tt=tt: s.activation(
                                Dd32[:, XR32[tt % 2]:XR32[tt % 2] + 2048],
                                Dd32[:, WR32[tt % 2]:WR32[tt % 2] + 2048], AF.Identity,
                                scale=SM32[:, S_RSTD:S_RSTD + 1],
                                bias=SM32[:, S_NMEAN:S_NMEAN + 1]),
                            [(VE, m_flnv)], AC)
                        m_y2st[tt] = sch.emit(
                            "sync",
                            lambda sy, tt=tt: sy.dma_start(
                                out=y_h[tt * 128:(tt + 1) * 128, :],
                                in_=Dd32[:, XR32[tt % 2]:XR32[tt % 2] + 2048]),
                            [(AC, m_fy2)], STT[tt], 16)
                m_eb_last = m_eb[(e, fh, NT - 1)]

        sch.emit("sync", lambda sy: sy.nop(),
                 [(STT[i], STT[i].n) for i in range(NT)])

        def replay(engname):
            def run(eng):
                seen = {}
                for waits, fn, inc, amt in sch.q[engname]:
                    for (c, v) in waits:
                        if v > seen.get(c, 0):
                            eng.wait_ge(c.h, v)
                            seen[c] = v
                    ins = fn(eng)
                    if inc is not None:
                        assert ins is not None
                        ins.then_inc(inc.h, amt)
            return run

        block.tensor(replay("tensor"))
        block.vector(replay("vector"))
        block.scalar(replay("scalar"))
        block.gpsimd(replay("gpsimd"))
        block.sync(replay("sync"))

    return nc


def _relayout(wq, wk, wv, wr, wg, wu):
    wq_r = np.ascontiguousarray(
        wq.reshape(ND, 128, NH, 128).transpose(1, 2, 0, 3).reshape(128, NH * ND * 128))
    wk_r = np.ascontiguousarray(
        wk.reshape(ND, 128, 512).transpose(1, 0, 2).reshape(128, ND * 512))
    wv_r = np.ascontiguousarray(
        wv.reshape(ND, 128, 512).transpose(1, 0, 2).reshape(128, ND * 512))
    wr_r = np.ascontiguousarray(
        wr.reshape(ND, 128, 8).transpose(1, 0, 2).reshape(128, ND * 8))
    wg_r = np.ascontiguousarray(
        wg.reshape(NE, ND, 128, NF, 128).transpose(0, 3, 2, 1, 4)
        .reshape(NE, NF, 128, ND * 128))
    wu_r = np.ascontiguousarray(
        wu.reshape(NE, ND, 128, NF, 128).transpose(0, 3, 2, 1, 4)
        .reshape(NE, NF, 128, ND * 128))
    return wq_r, wk_r, wv_r, wr_r, wg_r, wu_r


def kernel(L0_attn_x, L0_attn_Wq, L0_attn_Wk, L0_attn_Wv, L0_attn_Wo, L0_attn_bo,
           L0_ffn_Wr, L0_ffn_Wg, L0_ffn_Wu, L0_ffn_Wd):
    x = np.ascontiguousarray(np.asarray(L0_attn_x, dtype=np.float32))
    wq = np.ascontiguousarray(np.asarray(L0_attn_Wq, dtype=np.float32))
    wk = np.ascontiguousarray(np.asarray(L0_attn_Wk, dtype=np.float32))
    wv = np.ascontiguousarray(np.asarray(L0_attn_Wv, dtype=np.float32))
    wo = np.ascontiguousarray(np.asarray(L0_attn_Wo, dtype=np.float32))
    bo = np.ascontiguousarray(np.asarray(L0_attn_bo, dtype=np.float32)).reshape(1, D)
    wr = np.ascontiguousarray(np.asarray(L0_ffn_Wr, dtype=np.float32))
    wg = np.ascontiguousarray(np.asarray(L0_ffn_Wg, dtype=np.float32))
    wu = np.ascontiguousarray(np.asarray(L0_ffn_Wu, dtype=np.float32))
    wd = np.ascontiguousarray(np.asarray(L0_ffn_Wd, dtype=np.float32))

    wq_r, wk_r, wv_r, wr_r, wg_r, wu_r = _relayout(wq, wk, wv, wr, wg, wu)
    ident = np.eye(128, dtype=np.float32)
    mask = np.where(np.arange(128)[:, None] < np.arange(128)[None, :], NEG,
                    0.0).astype(np.float32)
    ones = np.ones((1, 128), dtype=np.float32)

    nc = build_program()
    in_maps = []
    for c in range(8):
        in_maps.append({
            "x": np.ascontiguousarray(x[c]),
            "wq": wq_r, "wk": wk_r, "wv": wv_r, "wo": wo, "bo": bo,
            "wr": wr_r, "wg": wg_r, "wu": wu_r, "wd": wd,
            "ident": ident, "mask": mask, "ones": ones,
        })
    res = run_bass_kernel_spmd(nc, in_maps, core_ids=list(range(8)))
    return np.stack([res.results[c]["y"] for c in range(8)], axis=0)


if __name__ == "__main__":
    nc = build_program()
    print("instructions:", nc.n_instructions())
